# version 02
# baseline (speedup 1.0000x reference)
import os
os.environ.setdefault("JAX_PLATFORMS", "cpu")
import numpy as np
import concourse.bass as bass
import concourse.mybir as mybir
from concourse.bass_utils import run_bass_kernel_spmd

F32 = mybir.dt.float32
I32 = mybir.dt.int32
AF = mybir.ActivationFunctionType
ALU = mybir.AluOpType

S = 2048
D = 1024
H = 16
DH = 64
HID = 3072
NCH = D // 128
NT = S // 128
NTB = S // 512
NJC = HID // 128
EPS_LN = 1e-5
EPS_RMS = 1.1920929e-07
QSCALE = 1.0 / 127.0
SHIFT = 20.0
NEG = -1.0e30

ENGINES = ("sync", "gpsimd", "tensor", "vector", "scalar")
DBG = False


class Script:
    def __init__(self):
        self.steps = {e: [] for e in ENGINES}
        self.cnt = {"DB": 0, "DQ": 0, "DWO": 0, "DX0": 0, "DX1": 0,
                    "DF0": 0, "DF1": 0, "DD0": 0, "DD1": 0,
                    "DO0": 0, "DO1": 0, "DT": 0,
                    "PE": 0, "VE": 0, "AC": 0}

    def v(self, name):
        return self.cnt[name]

    def add(self, eng, emit, waits=(), inc=None, inc_n=1):
        w = [(s, val) for (s, val) in waits if val > 0]
        self.steps[eng].append((emit, w, inc, inc_n))
        if inc is not None:
            self.cnt[inc] += inc_n


def build_nc():
    nc = bass.Bass("TRN2", target_bir_lowering=False, debug=False)

    xT = nc.dram_tensor("xT", [D, S], F32, kind="ExternalInput").ap()
    w2t = nc.dram_tensor("w2t", [D, 3 * D], F32, kind="ExternalInput").ap()
    b2 = nc.dram_tensor("b2", [1, 3 * D], F32, kind="ExternalInput").ap()
    wot = nc.dram_tensor("wot", [D, D], F32, kind="ExternalInput").ap()
    bo_d = nc.dram_tensor("bo", [1, D], F32, kind="ExternalInput").ap()
    wgt = nc.dram_tensor("wgt", [D, HID], I32, kind="ExternalInput").ap()
    wut = nc.dram_tensor("wut", [D, HID], I32, kind="ExternalInput").ap()
    wdt = nc.dram_tensor("wdt", [HID, D], I32, kind="ExternalInput").ap()
    ones_d = nc.dram_tensor("ones", [1, 512], F32, kind="ExternalInput").ap()
    onescol_d = nc.dram_tensor("onescol", [128, 1], F32, kind="ExternalInput").ap()
    zrow_d = nc.dram_tensor("zrow", [1, 128], F32, kind="ExternalInput").ap()
    mask_d = nc.dram_tensor("mask", [128, 128], F32, kind="ExternalInput").ap()
    ccols_d = nc.dram_tensor("ccols", [128, 2], F32, kind="ExternalInput").ap()
    outT = nc.dram_tensor("outT", [D, S], F32, kind="ExternalOutput").ap()
    taps = {}
    if DBG:
        for nm, shp in (("tap_rows", [1, 6144]), ("tap_normed", [128, 16384]),
                        ("tap_qt", [128, 4096]), ("tap_vt", [128, 2048]),
                        ("tap_z", [128, 2048]), ("tap_s", [128, 16384]),
                        ("tap_xn", [128, 4096]), ("tap_h", [128, 12288]),
                        ("tap_phat", [128, 2048]), ("tap_zp", [128, 8]),
                        ("tap_xn_all", [128, 16384]), ("tap_h_all", [128, 49152]),
                        ("tap_g_all", [128, 16384])):
            taps[nm] = nc.dram_tensor(nm, shp, F32, kind="ExternalOutput").ap()

    sc = Script()

    from contextlib import ExitStack
    with ExitStack() as stack:
        ent = stack.enter_context
        R1h = ent(nc.sbuf_tensor("R1", [128, 16384], F32))
        R2h = ent(nc.sbuf_tensor("R2", [128, 16384], F32))
        R3h = ent(nc.sbuf_tensor("R3", [128, 4096], F32))
        R4h = ent(nc.sbuf_tensor("R4", [128, 2048], F32))
        R5h = ent(nc.sbuf_tensor("R5", [128, 2048], F32))
        R6h = ent(nc.sbuf_tensor("R6", [128, 4096], F32))
        row_m_h = ent(nc.sbuf_tensor("row_m", [1, 2048], F32))
        row_istd_h = ent(nc.sbuf_tensor("row_istd", [1, 2048], F32))
        row_c2_h = ent(nc.sbuf_tensor("row_c2", [1, 2048], F32))
        bo_sb_h = ent(nc.sbuf_tensor("bo_sb", [1, 1024], F32))
        zparts_h = ent(nc.sbuf_tensor("zparts", [128, 4], F32))
        zcol_h = ent(nc.sbuf_tensor("zcol", [128, 2], F32))
        mcol_h = ent(nc.sbuf_tensor("mcol", [128, 8], F32))
        ones_sb_h = ent(nc.sbuf_tensor("ones_sb", [1, 512], F32))
        onescol_sb_h = ent(nc.sbuf_tensor("onescol_sb", [128, 1], F32))
        zrow_sb_h = ent(nc.sbuf_tensor("zrow_sb", [1, 128], F32))
        ccols_h = ent(nc.sbuf_tensor("ccols_sb", [128, 2], F32))
        pool_h = ent(nc.psum_tensor("pool", [128, 4096], F32))
        semmap = {}
        for nm in ("DB", "DQ", "DWO", "DX0", "DX1", "DF0", "DF1",
                   "DD0", "DD1", "DO0", "DO1", "DT", "PE", "VE", "AC"):
            semmap[nm] = ent(nc.semaphore("s" + nm))
        block = ent(nc.Block())
        R1, R2, R3 = R1h[:, :], R2h[:, :], R3h[:, :]
        R4, R5, R6 = R4h[:, :], R5h[:, :], R6h[:, :]
        row_m, row_istd, row_c2 = row_m_h[:, :], row_istd_h[:, :], row_c2_h[:, :]
        bo_sb = bo_sb_h[:, :]
        zparts, zcol = zparts_h[:, :], zcol_h[:, :]
        mcol = mcol_h[:, :]
        ones_sb = ones_sb_h[:, :]
        onescol, zrow_sb = onescol_sb_h[:, :], zrow_sb_h[:, :]
        negshift = ccols_h[:, 0:1]
        qscale_col = ccols_h[:, 1:2]
        mask_sb = R6[:, 3200:3328]
        bias_buf = R6[0:1, 3328:3840]
        P = pool_h[:, :]
        sems = semmap

        def bank(b, w=512):
            return P[:, b * 512: b * 512 + w]

        sc.add("sync", lambda e: e.dma_start(out=ones_sb, in_=ones_d), inc="DB", inc_n=16)
        sc.add("sync", lambda e: e.dma_start(out=onescol, in_=onescol_d), inc="DB", inc_n=16)
        sc.add("sync", lambda e: e.dma_start(out=zrow_sb, in_=zrow_d), inc="DB", inc_n=16)
        sc.add("sync", lambda e: e.dma_start(out=mask_sb, in_=mask_d), inc="DB", inc_n=16)
        sc.add("sync", lambda e: e.dma_start(out=bo_sb, in_=bo_d), inc="DB", inc_n=16)
        sc.add("sync", lambda e: e.dma_start(out=ccols_h[:, :], in_=ccols_d), inc="DB", inc_n=16)
        sc.add("sync",
               lambda e: e.dma_start(out=R2.rearrange("p (c t) -> p c t", c=NCH),
                                     in_=xT.rearrange("(c p) t -> p c t", p=128)),
               inc="DB", inc_n=16)
        dw_boot = sc.v("DB")

        for tb in range(NTB):
            for c in range(NCH):
                last = (tb == NTB - 1 and c == NCH - 1)
                sc.add("tensor",
                       lambda e, tb=tb, c=c: e.matmul(
                           P[0:1, tb * 512:(tb + 1) * 512],
                           onescol,
                           R2[:, c * 2048 + tb * 512: c * 2048 + (tb + 1) * 512],
                           start=(c == 0), stop=(c == NCH - 1)),
                       waits=[("DB", dw_boot)] if (tb == 0 and c == 0) else (),
                       inc="PE" if last else None)
        pe_sum = sc.v("PE")

        sq_slot_pe = {}
        for tb in range(NTB):
            for c in range(NCH):
                slot = (tb * NCH + c) % 4
                sc.add("scalar",
                       lambda e, c=c, tb=tb, slot=slot: e.activation(
                           R4[:, slot * 512:(slot + 1) * 512],
                           R2[:, c * 2048 + tb * 512: c * 2048 + (tb + 1) * 512],
                           AF.Square),
                       waits=([("DB", dw_boot)] if (c == 0 and tb == 0) else [])
                       + [("PE", sq_slot_pe.get(slot, 0))],
                       inc="AC")
                ac_sq = sc.v("AC")
                sc.add("tensor",
                       lambda e, c=c, tb=tb, slot=slot: e.matmul(
                           P[0:1, 2048 + tb * 512: 2048 + (tb + 1) * 512],
                           onescol,
                           R4[:, slot * 512:(slot + 1) * 512],
                           start=(c == 0), stop=(c == NCH - 1)),
                       waits=[("AC", ac_sq)],
                       inc="PE")
                sq_slot_pe[slot] = sc.v("PE")
        pe_sumsq = sc.v("PE")

        rowC = R5[0:1, 0:2048]
        rowD = R4[0:1, 0:2048]

        def vadd(emit, waits=()):
            sc.add("vector", emit, list(waits) + [("VE", sc.v("VE"))], inc="VE")

        vadd(lambda e: e.tensor_copy(row_m, P[0:1, 0:2048]), [("PE", pe_sum)])
        vadd(lambda e: e.tensor_scalar_mul(row_m, row_m, 1.0 / D))
        vadd(lambda e: e.tensor_copy(rowD, P[0:1, 2048:4096]), [("PE", pe_sumsq)])
        vadd(lambda e: e.tensor_scalar_mul(rowD, rowD, 1.0 / D))
        vadd(lambda e: e.tensor_mul(rowC, row_m, row_m))
        vadd(lambda e: e.tensor_sub(rowD, rowD, rowC))
        vadd(lambda e: e.tensor_scalar_add(rowC, rowD, EPS_LN))
        sc.add("scalar", lambda e: e.activation(row_istd, rowC, AF.Sqrt),
               [("VE", sc.v("VE"))], inc="AC")
        vadd(lambda e: e.reciprocal(row_c2, row_istd), [("AC", sc.v("AC"))])
        vadd(lambda e: e.tensor_mul(row_istd, rowC, row_c2))
        vadd(lambda e: e.tensor_mul(row_istd, row_istd, row_c2))
        vadd(lambda e: e.tensor_scalar(row_istd, row_istd, -0.5, 1.5,
                                       ALU.mult, ALU.add))
        vadd(lambda e: e.tensor_mul(row_istd, row_c2, row_istd))
        vadd(lambda e: e.reciprocal(row_c2, rowC))
        vadd(lambda e: e.tensor_mul(rowD, rowD, row_c2))
        vadd(lambda e: e.tensor_scalar_add(rowD, rowD, EPS_RMS))
        sc.add("scalar", lambda e: e.activation(rowC, rowD, AF.Sqrt),
               [("VE", sc.v("VE"))], inc="AC")
        vadd(lambda e: e.reciprocal(row_c2, rowC), [("AC", sc.v("AC"))])
        vadd(lambda e: e.tensor_mul(rowC, rowD, row_c2))
        vadd(lambda e: e.tensor_mul(rowC, rowC, row_c2))
        vadd(lambda e: e.tensor_scalar(rowC, rowC, -0.5, 1.5, ALU.mult, ALU.add))
        vadd(lambda e: e.tensor_mul(row_c2, row_c2, rowC))
        vadd(lambda e: e.tensor_mul(row_c2, row_c2, row_istd))
        ve_rows = sc.v("VE")

        ve_prev_tb = 0
        for tb in range(NTB):
            sc.add("tensor",
                   lambda e, tb=tb: e.matmul(
                       bank(4), ones_sb[0:1, 0:128],
                       row_m[0:1, tb * 512:(tb + 1) * 512], start=True, stop=True),
                   waits=[("VE", ve_rows), ("VE", ve_prev_tb)])
            sc.add("tensor",
                   lambda e, tb=tb: e.matmul(
                       bank(5), ones_sb[0:1, 0:128],
                       row_c2[0:1, tb * 512:(tb + 1) * 512], start=True, stop=True),
                   inc="PE")
            pe_bc = sc.v("PE")
            for c in range(NCH):
                sc.add("vector",
                       lambda e, tb=tb, c=c: e.tensor_sub(
                           R1[:, c * 2048 + tb * 512: c * 2048 + (tb + 1) * 512],
                           R2[:, c * 2048 + tb * 512: c * 2048 + (tb + 1) * 512],
                           bank(4)),
                       waits=[("PE", pe_bc)], inc="VE")
                sc.add("vector",
                       lambda e, tb=tb, c=c: e.tensor_mul(
                           R1[:, c * 2048 + tb * 512: c * 2048 + (tb + 1) * 512],
                           R1[:, c * 2048 + tb * 512: c * 2048 + (tb + 1) * 512],
                           bank(5)),
                       waits=[("VE", sc.v("VE"))], inc="VE")
            ve_prev_tb = sc.v("VE")
        ve_normed = sc.v("VE")
        if DBG:
            sc.add("sync", lambda e: e.dma_start(out=taps["tap_rows"][0:1, 0:2048], in_=row_m),
                   waits=[("VE", ve_rows)], inc="DT", inc_n=16)
            sc.add("sync", lambda e: e.dma_start(out=taps["tap_rows"][0:1, 2048:4096], in_=row_istd),
                   inc="DT", inc_n=16)
            sc.add("sync", lambda e: e.dma_start(out=taps["tap_rows"][0:1, 4096:6144], in_=row_c2),
                   inc="DT", inc_n=16)
            sc.add("sync", lambda e: e.dma_start(out=taps["tap_normed"], in_=R1),
                   waits=[("VE", ve_normed)], inc="DT", inc_n=16)

        pe_end_pair = 0
        ve_z_evac = 0
        ac_bank01 = 0
        ac_bank23 = 0
        ve_s_acc = {0: 0, 1: 0}
        qk_bank_free = {0: 0, 1: 0, 2: 0, 3: 0}
        pe_scores_end = 0
        pe_attn_all = 0

        dbg_gate = {}
        for p in range(8):
            sc.add("sync",
                   lambda e, p=p: e.dma_start(
                       out=R6[:, 0:1024].rearrange("p (c j) -> p c j", j=128),
                       in_=w2t[:, p * 128:(p + 1) * 128].rearrange("(c p) j -> p c j", p=128)),
                   waits=[("PE", pe_end_pair), ("AC", ac_bank01)],
                   inc="DQ", inc_n=16)
            sc.add("sync",
                   lambda e, p=p: e.dma_start(
                       out=R6[:, 1024:2048].rearrange("p (c j) -> p c j", j=128),
                       in_=w2t[:, 1024 + p * 128: 1024 + (p + 1) * 128]
                       .rearrange("(c p) j -> p c j", p=128)),
                   inc="DQ", inc_n=16)
            sc.add("sync",
                   lambda e, p=p: e.dma_start(
                       out=R6[:, 2048:3072].rearrange("p (c j) -> p c j", j=128),
                       in_=w2t[:, 2048 + p * 128: 2048 + (p + 1) * 128]
                       .rearrange("(c p) j -> p c j", p=128)),
                   inc="DQ", inc_n=16)
            for ci in range(3):
                sc.add("sync",
                       lambda e, p=p, ci=ci: e.dma_start(
                           out=bias_buf[0:1, ci * 128:(ci + 1) * 128],
                           in_=b2[0:1, ci * 1024 + p * 128: ci * 1024 + (p + 1) * 128]),
                       inc="DQ", inc_n=16)
            dw_qkvw = sc.v("DQ")

            pe_q_done = 0
            for which, woff, boff, r3off in (("q", 0, 0, 0), ("k", 1024, 128, 2048)):
                for tb in range(NTB):
                    bq = tb % 2
                    for c in range(NCH):
                        sc.add("tensor",
                               lambda e, woff=woff, c=c, tb=tb, bq=bq: e.matmul(
                                   bank(bq), R6[:, woff + c * 128: woff + (c + 1) * 128],
                                   R1[:, c * 2048 + tb * 512: c * 2048 + (tb + 1) * 512],
                                   start=(c == 0), stop=False),
                               waits=([("DQ", dw_qkvw), ("VE", ve_normed),
                                       ("VE", qk_bank_free[bq])] if c == 0 else ()))
                    sc.add("tensor",
                           lambda e, boff=boff, bq=bq: e.matmul(
                               bank(bq), bias_buf[0:1, boff:boff + 128],
                               ones_sb[0:1, 0:512], start=False, stop=True),
                           inc="PE")
                    pe_proj = sc.v("PE")
                    sc.add("vector",
                           lambda e, r3off=r3off, tb=tb, bq=bq: e.tensor_copy(
                               R3[:, r3off + tb * 512: r3off + (tb + 1) * 512], bank(bq)),
                           waits=[("PE", pe_proj)], inc="VE")
                    qk_bank_free[bq] = sc.v("VE")
                if which == "q":
                    pe_q_done = sc.v("PE")
            sc.add("sync",
                   lambda e, p=p: e.dma_start(out=R6[:, 0:1024],
                                              in_=wot[p * 128:(p + 1) * 128, :]),
                   waits=[("PE", pe_q_done)], inc="DWO", inc_n=16)
            dw_wo = sc.v("DWO")

            for tt in range(NT):
                bv = 2 + tt % 2
                for c in range(NCH):
                    sc.add("tensor",
                           lambda e, c=c, tt=tt, bv=bv: e.matmul(
                               bank(bv, 128),
                               R1[:, c * 2048 + tt * 128: c * 2048 + (tt + 1) * 128],
                               R6[:, 2048 + c * 128: 2048 + (c + 1) * 128],
                               start=(c == 0), stop=False),
                           waits=[("VE", qk_bank_free[bv])] if c == 0 else ())
                sc.add("tensor",
                       lambda e, bv=bv: e.matmul(
                           bank(bv, 128), ones_sb[0:1, 0:128],
                           bias_buf[0:1, 256:384], start=False, stop=True),
                       inc="PE")
                pe_v = sc.v("PE")
                sc.add("vector",
                       lambda e, tt=tt, bv=bv: e.tensor_copy(
                           R4[:, tt * 128:(tt + 1) * 128], bank(bv, 128)),
                       waits=[("PE", pe_v)], inc="VE")
                qk_bank_free[bv] = sc.v("VE")

            if DBG and p == 0:
                sc.add("sync", lambda e: e.dma_start(out=taps["tap_qt"], in_=R3),
                       waits=[("VE", sc.v("VE"))], inc="DT", inc_n=16)
                sc.add("sync", lambda e: e.dma_start(out=taps["tap_vt"], in_=R4),
                       inc="DT", inc_n=16)
            for qb in range(4):
                sc.add("tensor",
                       lambda e, qb=qb: e.matmul(
                           bank(4 + qb), zrow_sb, ones_sb[0:1, 0:512],
                           start=True, stop=True),
                       waits=([("VE", ve_z_evac)] if qb == 0 else ()),
                       inc="PE" if qb == 3 else None)

            pe_avA = sc.v("PE")
            pe_avB = sc.v("PE")
            ve_vsA_prev = 0
            ve_vsB_prev = 0
            ac_mcol6 = 0
            for kc in range(NT):
                k0 = kc * 128
                halves = [(k0 + sh * 1024, min(S, k0 + (sh + 1) * 1024))
                          for sh in range(2) if k0 + sh * 1024 < S]
                nh = len(halves)
                pe_sc = {}
                ac_exp = {}
                for hx, rows, xref in (("A", (0, 64), 0), ("B", (64, 128), 1)):
                    nblk = 0
                    for sh, (qlo, qhi) in enumerate(halves):
                        coff = sh * 1024
                        first_waits = []
                        if hx == "A":
                            first_waits.append(
                                ("AC", ac_bank01 if sh == 0 else ac_bank23))
                        else:
                            first_waits.append(("AC", ac_exp[("A", sh)]))
                        qs = qlo
                        while qs < qhi:
                            w = min(512, qhi - qs)
                            lastmm = (qs + w >= qhi)
                            sc.add("tensor",
                                   lambda e, rows=rows, coff=coff, qs=qs, w=w,
                                   qlo=qlo, k0=k0: e.matmul(
                                       P[:, coff + (qs - qlo): coff + (qs - qlo) + w],
                                       R3[rows[0]:rows[1], 2048 + k0: 2048 + k0 + 128],
                                       R3[rows[0]:rows[1], qs: qs + w],
                                       start=True, stop=True),
                                   waits=first_waits if qs == qlo else (),
                                   inc="PE" if lastmm else None)
                            qs += w
                        pe_sc[(hx, sh)] = sc.v("PE")
                        if sh == 0:
                            sc.add("vector",
                                   lambda e, coff=coff: e.tensor_add(
                                       P[:, coff:coff + 128],
                                       P[:, coff:coff + 128], mask_sb),
                                   waits=[("PE", pe_sc[(hx, sh)])], inc="VE")
                        qs = qlo
                        while qs < qhi:
                            w = min(512, qhi - qs)
                            sc.add("vector",
                                   lambda e, coff=coff, qs=qs, w=w, qlo=qlo,
                                   nblk=nblk: e.reduce_max(
                                       mcol[:, nblk:nblk + 1],
                                       P[:, coff + (qs - qlo): coff + (qs - qlo) + w],
                                       axis=mybir.AxisListType.X),
                                   waits=[("PE", pe_sc[(hx, sh)])], inc="VE")
                            nblk += 1
                            qs += w
                    sc.add("vector",
                           lambda e, nblk=nblk: e.reduce_max(
                               mcol[:, 6:7], mcol[:, 0:nblk],
                               axis=mybir.AxisListType.X, negate=True),
                           waits=[("AC", ac_mcol6), ("VE", sc.v("VE"))],
                           inc="VE")
                    ve_negm = sc.v("VE")
                    for sh, (qlo, qhi) in enumerate(halves):
                        coff = sh * 1024
                        zp = (0 if hx == "A" else 2) + sh
                        ewaits = [("VE", ve_negm),
                                  ("PE", pe_avA if hx == "B" else pe_avB),
                                  ("VE", ve_vsA_prev if hx == "A" else ve_vsB_prev),
                                  ("DT", dbg_gate.get("phat", 0))]
                        sc.add("scalar",
                               lambda e, coff=coff, qlo=qlo, qhi=qhi, zp=zp: e.activation(
                                   R5[:, qlo:qhi], P[:, coff:coff + (qhi - qlo)],
                                   AF.Exp, bias=mcol[:, 6:7],
                                   accum_out=zparts[:, zp:zp + 1]),
                               waits=ewaits, inc="AC")
                        ac_exp[(hx, sh)] = sc.v("AC")
                        if DBG and p == 0 and kc == 0 and hx == "A" and sh == nh - 1:
                            sc.add("sync", lambda e: e.dma_start(out=taps["tap_phat"], in_=R5),
                                   waits=[("AC", sc.v("AC"))], inc="DT", inc_n=16)
                            dbg_gate["phat"] = sc.v("DT")
                        if sh == 0:
                            ac_bank01 = sc.v("AC")
                        else:
                            ac_bank23 = sc.v("AC")
                    ac_mcol6 = sc.v("AC")
                    zlo = 0 if hx == "A" else 2
                    voff = 3072 if hx == "A" else 3136
                    vcol = 0 if hx == "A" else 64
                    if nh == 2:
                        sc.add("vector",
                               lambda e, zlo=zlo: e.tensor_add(
                                   zcol[:, 0:1], zparts[:, zlo:zlo + 1],
                                   zparts[:, zlo + 1:zlo + 2]),
                               waits=[("AC", ac_exp[(hx, nh - 1)]),
                                      ("DT", dbg_gate.get("zp", 0))], inc="VE")
                    else:
                        sc.add("vector",
                               lambda e, zlo=zlo: e.tensor_copy(
                                   zcol[:, 0:1], zparts[:, zlo:zlo + 1]),
                               waits=[("AC", ac_exp[(hx, nh - 1)])], inc="VE")
                    sc.add("vector", lambda e: e.reciprocal(zcol[:, 1:2], zcol[:, 0:1]),
                           waits=[("VE", sc.v("VE"))], inc="VE")
                    sc.add("vector",
                           lambda e, voff=voff, k0=k0, vcol=vcol: e.tensor_scalar_mul(
                               R6[:, voff:voff + 64],
                               R4[:, k0 + vcol: k0 + vcol + 64],
                               zcol[:, 1:2]),
                           waits=[("PE", pe_avA if hx == "A" else pe_avB),
                                  ("VE", sc.v("VE"))],
                           inc="VE")
                    ve_vs = sc.v("VE")
                    if DBG and p == 0 and kc == 0 and hx == "A":
                        sc.add("sync", lambda e: e.dma_start(
                                   out=taps["tap_zp"][:, 0:4], in_=zparts),
                               waits=[("VE", ve_vs)], inc="DT", inc_n=16)
                        sc.add("sync", lambda e: e.dma_start(
                                   out=taps["tap_zp"][:, 4:6], in_=zcol),
                               inc="DT", inc_n=16)
                        dbg_gate["zp"] = sc.v("DT")
                    if hx == "A":
                        ve_vsA_prev = ve_vs
                    else:
                        ve_vsB_prev = ve_vs
                    prange = (0, 64) if hx == "A" else (64, 128)
                    tp = None if hx == "A" else (0, 64)
                    qbs = list(range(k0 // 512, 4))
                    for i, qb in enumerate(qbs):
                        qs = max(qb * 512, k0)
                        w = (qb + 1) * 512 - qs
                        sc.add("tensor",
                               lambda e, prange=prange, qs=qs, w=w, voff=voff, tp=tp:
                               e.matmul(
                                   P[prange[0]:prange[1], 2048 + qs: 2048 + qs + w],
                                   R6[:, voff:voff + 64],
                                   R5[:, qs:qs + w],
                                   start=False, stop=False, tile_position=tp),
                               waits=([("VE", ve_vs), ("AC", ac_exp[(hx, nh - 1)])]
                                      if i == 0 else ()),
                               inc="PE" if i == len(qbs) - 1 else None)
                    if hx == "A":
                        pe_avA = sc.v("PE")
                    else:
                        pe_avB = sc.v("PE")
            pe_scores_end = sc.v("PE")

            sc.add("vector",
                   lambda e: e.tensor_copy(R4[:, 0:2048], P[:, 2048:4096]),
                   waits=[("PE", pe_avB)], inc="VE")
            ve_z_evac = sc.v("VE")
            if DBG and p == 0:
                sc.add("sync", lambda e: e.dma_start(out=taps["tap_z"], in_=R4),
                       waits=[("VE", ve_z_evac)], inc="DT", inc_n=16)
            for dc in range(NCH):
                for tb in range(NTB):
                    bb = (dc * NTB + tb) % 2
                    if p == 0:
                        sc.add("tensor",
                               lambda e, dc=dc, tb=tb, bb=bb: e.matmul(
                                   bank(bb), R6[:, dc * 128:(dc + 1) * 128],
                                   R4[:, tb * 512:(tb + 1) * 512],
                                   start=True, stop=False),
                               waits=[("VE", ve_z_evac), ("DWO", dw_wo),
                                      ("VE", ve_s_acc[bb]), ("AC", ac_bank01),
                                      ("AC", ac_bank23)])
                        sc.add("tensor",
                               lambda e, dc=dc, bb=bb: e.matmul(
                                   bank(bb), bo_sb[0:1, dc * 128:(dc + 1) * 128],
                                   ones_sb[0:1, 0:512], start=False, stop=True),
                               inc="PE")
                    else:
                        sc.add("tensor",
                               lambda e, dc=dc, tb=tb, bb=bb: e.matmul(
                                   bank(bb), R6[:, dc * 128:(dc + 1) * 128],
                                   R4[:, tb * 512:(tb + 1) * 512],
                                   start=True, stop=True),
                               waits=[("VE", ve_z_evac), ("DWO", dw_wo),
                                      ("VE", ve_s_acc[bb]), ("AC", ac_bank01),
                                      ("AC", ac_bank23)],
                               inc="PE")
                    pe_wo = sc.v("PE")
                    if p == 0:
                        sc.add("vector",
                               lambda e, dc=dc, tb=tb, bb=bb: e.tensor_copy(
                                   R2[:, dc * 2048 + tb * 512: dc * 2048 + (tb + 1) * 512],
                                   bank(bb)),
                               waits=[("PE", pe_wo)], inc="VE")
                    else:
                        sc.add("vector",
                               lambda e, dc=dc, tb=tb, bb=bb: e.tensor_add(
                                   R2[:, dc * 2048 + tb * 512: dc * 2048 + (tb + 1) * 512],
                                   R2[:, dc * 2048 + tb * 512: dc * 2048 + (tb + 1) * 512],
                                   bank(bb)),
                               waits=[("PE", pe_wo)], inc="VE")
                    ve_s_acc[bb] = sc.v("VE")
            pe_end_pair = sc.v("PE")
        pe_attn_all = sc.v("PE")
        if DBG:
            sc.add("sync", lambda e: e.dma_start(out=taps["tap_s"], in_=R2),
                   waits=[("VE", sc.v("VE"))], inc="DT", inc_n=16)

        ffn_slot_free = {0: pe_attn_all, 1: pe_attn_all}
        wd_free = {0: pe_attn_all, 1: pe_scores_end}
        r3slot_ve = {0: 0, 1: 0}
        ve_xn_done = 0
        pe_xn_readers = 0
        h_done = {}
        ve_comb = {}
        do_out = {}
        ac_g_prev = 0
        oidx = 0
        for tb in range(NTB):
            sc.add("tensor",
                   lambda e, tb=tb: e.matmul(
                       bank(6), ones_sb[0:1, 0:128],
                       row_m[0:1, tb * 512:(tb + 1) * 512], start=True, stop=True),
                   waits=[("VE", ve_z_evac), ("VE", ve_xn_done)])
            sc.add("tensor",
                   lambda e, tb=tb: e.matmul(
                       bank(7), ones_sb[0:1, 0:128],
                       row_istd[0:1, tb * 512:(tb + 1) * 512], start=True, stop=True),
                   inc="PE")
            pe_bc4 = sc.v("PE")
            for c in range(NCH):
                slot = c % 2
                sc.add("sync",
                       lambda e, c=c, tb=tb, slot=slot: e.dma_start(
                           out=R3[:, slot * 512:(slot + 1) * 512],
                           in_=xT[c * 128:(c + 1) * 128, tb * 512:(tb + 1) * 512]),
                       waits=[("PE", pe_scores_end), ("VE", r3slot_ve[slot])],
                       inc="DX%d" % slot, inc_n=16)
                dw_x2 = ("DX%d" % slot, sc.v("DX%d" % slot))
                sc.add("vector",
                       lambda e, c=c, slot=slot: e.tensor_sub(
                           R1[:, 12288 + c * 512: 12288 + (c + 1) * 512],
                           R3[:, slot * 512:(slot + 1) * 512], bank(6)),
                       waits=[dw_x2, ("PE", pe_bc4), ("PE", pe_attn_all),
                              ("PE", pe_xn_readers), ("DT", dbg_gate.get("xn", 0))],
                       inc="VE")
                sc.add("vector",
                       lambda e, c=c: e.tensor_mul(
                           R1[:, 12288 + c * 512: 12288 + (c + 1) * 512],
                           R1[:, 12288 + c * 512: 12288 + (c + 1) * 512], bank(7)),
                       waits=[("VE", sc.v("VE"))], inc="VE")
                r3slot_ve[slot] = sc.v("VE")
            ve_xn_done = sc.v("VE")
            if DBG:
                sc.add("sync", lambda e, tb=tb: e.dma_start(
                           out=taps["tap_xn_all"][:, tb * 4096:(tb + 1) * 4096],
                           in_=R1[:, 12288:16384]),
                       waits=[("VE", ve_xn_done)], inc="DT", inc_n=16)
                dbg_gate["xn"] = sc.v("DT")
            if DBG and tb == 0:
                sc.add("sync", lambda e: e.dma_start(out=taps["tap_xn"], in_=R1[:, 12288:16384]),
                       waits=[("VE", ve_xn_done)], inc="DT", inc_n=16)

            for jb in range(NJC):
                slot = jb % 2
                soff = 2048 * slot
                sc.add("gpsimd",
                       lambda e, jb=jb, soff=soff: e.dma_start(
                           out=R6[:, soff:soff + 1024].rearrange("p (c j) -> p c j", j=128),
                           in_=wgt[:, jb * 128:(jb + 1) * 128]
                           .rearrange("(c p) j -> p c j", p=128)),
                       waits=[("PE", ffn_slot_free[slot])],
                       inc="DF%d" % slot, inc_n=16)
                sc.add("gpsimd",
                       lambda e, jb=jb, soff=soff: e.dma_start(
                           out=R6[:, soff + 1024:soff + 2048]
                           .rearrange("p (c j) -> p c j", j=128),
                           in_=wut[:, jb * 128:(jb + 1) * 128]
                           .rearrange("(c p) j -> p c j", p=128)),
                       inc="DF%d" % slot, inc_n=16)
                dg_w = ("DF%d" % slot, sc.v("DF%d" % slot))
                for c in range(NCH):
                    sc.add("tensor",
                           lambda e, c=c, soff=soff, slot=slot: e.matmul(
                               bank(slot), R6[:, soff + c * 128: soff + (c + 1) * 128],
                               R1[:, 12288 + c * 512: 12288 + (c + 1) * 512],
                               start=(c == 0), stop=(c == NCH - 1)),
                           waits=([dg_w, ("VE", ve_xn_done),
                                   ("AC", ac_g_prev if jb < 2 else 0),
                                   ("VE", h_done.get(jb - 2, 0))]
                                  if c == 0 else ()),
                           inc="PE" if c == NCH - 1 else None)
                pe_gate = sc.v("PE")
                for c in range(NCH):
                    sc.add("tensor",
                           lambda e, c=c, soff=soff, slot=slot: e.matmul(
                               bank(2 + slot),
                               R6[:, soff + 1024 + c * 128: soff + 1024 + (c + 1) * 128],
                               R1[:, 12288 + c * 512: 12288 + (c + 1) * 512],
                               start=(c == 0), stop=(c == NCH - 1)),
                           inc="PE" if c == NCH - 1 else None)
                pe_up = sc.v("PE")
                pe_xn_readers = pe_up
                ffn_slot_free[slot] = pe_up
                sc.add("scalar",
                       lambda e, slot=slot: e.activation(
                           R5[:, 1024:1536], bank(slot), AF.Silu, scale=qscale_col),
                       waits=[("PE", pe_gate), ("VE", h_done.get(jb - 1, 0))],
                       inc="AC")
                ac_silu = sc.v("AC")
                sc.add("vector",
                       lambda e, jb=jb, slot=slot: e.scalar_tensor_tensor(
                           R1[:, jb * 512:(jb + 1) * 512], bank(2 + slot), QSCALE,
                           R5[:, 1024:1536], ALU.mult, ALU.mult),
                       waits=[("AC", ac_silu), ("PE", pe_up),
                              ("DT", dbg_gate.get("h", 0))], inc="VE")
                h_done[jb] = sc.v("VE")
            ve_h_all = sc.v("VE")
            if DBG:
                sc.add("sync", lambda e, tb=tb: e.dma_start(
                           out=taps["tap_h_all"][:, tb * 12288:(tb + 1) * 12288],
                           in_=R1[:, 0:12288]),
                       waits=[("VE", ve_h_all)], inc="DT", inc_n=16)
                dbg_gate["h"] = sc.v("DT")
            if DBG and tb == 0:
                sc.add("sync", lambda e: e.dma_start(out=taps["tap_h"], in_=R1[:, 0:12288]),
                       waits=[("VE", ve_h_all)], inc="DT", inc_n=16)
            ffn_slot_free = dict(ffn_slot_free)

            for dc in range(NCH):
                wslot = dc % 2
                wd_dst = R6[:, 0:3072] if wslot == 0 else R3[:, 1024:4096]
                base_gate = ([("PE", ffn_slot_free[0]), ("PE", ffn_slot_free[1])]
                             if wslot == 0 else [("VE", ve_xn_done)])
                for half in range(2):
                    sc.add("gpsimd",
                           lambda e, dc=dc, half=half, wd_dst=wd_dst: e.dma_start(
                               out=wd_dst[:, half * 1536:(half + 1) * 1536]
                               .rearrange("p (j d) -> p j d", d=128),
                               in_=wdt[half * 1536:(half + 1) * 1536,
                                       dc * 128:(dc + 1) * 128]
                               .rearrange("(j p) d -> p j d", p=128)),
                           waits=(base_gate + [("PE", wd_free[wslot])])
                           if half == 0 else (),
                           inc="DD%d" % wslot, inc_n=16)
                dg_wd = ("DD%d" % wslot, sc.v("DD%d" % wslot))
                xslot = dc % 2
                sc.add("sync",
                       lambda e, dc=dc, tb=tb, xslot=xslot: e.dma_start(
                           out=R3[:, xslot * 512:(xslot + 1) * 512],
                           in_=xT[dc * 128:(dc + 1) * 128, tb * 512:(tb + 1) * 512]),
                       waits=[("VE", ve_xn_done), ("VE", r3slot_ve[xslot])],
                       inc="DX%d" % xslot, inc_n=16)
                dw_xc = ("DX%d" % xslot, sc.v("DX%d" % xslot))
                for jc in range(NJC):
                    sc.add("tensor",
                           lambda e, jc=jc, wd_dst=wd_dst, dc=dc: e.matmul(
                               bank(4 + dc % 2), wd_dst[:, jc * 128:(jc + 1) * 128],
                               R1[:, jc * 512:(jc + 1) * 512],
                               start=(jc == 0), stop=(jc == NJC - 1)),
                           waits=([dg_wd, ("VE", ve_h_all),
                                   ("VE", ve_comb.get(oidx - 2, 0))]
                                  if jc == 0 else ()),
                           inc="PE" if jc == NJC - 1 else None)
                pe_y = sc.v("PE")
                wd_free[wslot] = pe_y
                sc.add("scalar",
                       lambda e, dc=dc: e.activation(
                           R5[:, 1536:2048], bank(4 + dc % 2), AF.Silu, scale=qscale_col),
                       waits=[("PE", pe_y), ("VE", ve_comb.get(oidx - 1, 0))],
                       inc="AC")
                ac_g = sc.v("AC")
                ac_g_prev = ac_g
                if DBG:
                    sc.add("sync", lambda e, tb=tb, dc=dc: e.dma_start(
                               out=taps["tap_g_all"][:, (tb * 8 + dc) * 512:
                                                     (tb * 8 + dc + 1) * 512],
                               in_=R5[:, 1536:2048]),
                           waits=[("AC", ac_g)], inc="DT", inc_n=16)
                    dbg_gate["g"] = sc.v("DT")
                oslot = dc % 2
                sc.add("vector",
                       lambda e, dc=dc, tb=tb, oslot=oslot: e.tensor_mul(
                           R5[:, oslot * 512:(oslot + 1) * 512],
                           R2[:, dc * 2048 + tb * 512: dc * 2048 + (tb + 1) * 512],
                           R5[:, 1536:2048]),
                       waits=[do_out.get(oidx - 2, ("DO0", 0)),
                              ("DT", dbg_gate.get("g", 0)), ("AC", ac_g)],
                       inc="VE")
                sc.add("vector",
                       lambda e, oslot=oslot, xslot=xslot: e.tensor_add(
                           R5[:, oslot * 512:(oslot + 1) * 512],
                           R5[:, oslot * 512:(oslot + 1) * 512],
                           R3[:, xslot * 512:(xslot + 1) * 512]),
                       waits=[dw_xc, ("VE", sc.v("VE"))], inc="VE")
                ve_comb[oidx] = sc.v("VE")
                r3slot_ve[xslot] = sc.v("VE")
                sc.add("sync",
                       lambda e, dc=dc, tb=tb, oslot=oslot: e.dma_start(
                           out=outT[dc * 128:(dc + 1) * 128, tb * 512:(tb + 1) * 512],
                           in_=R5[:, oslot * 512:(oslot + 1) * 512]),
                       waits=[("VE", ve_comb[oidx])], inc="DO%d" % oslot, inc_n=16)
                do_out[oidx] = ("DO%d" % oslot, sc.v("DO%d" % oslot))
                oidx += 1
            ffn_slot_free = {0: sc.v("PE"), 1: sc.v("PE")}
            wd_free = {0: sc.v("PE"), 1: sc.v("PE")}

        sc.add("sync", None, waits=[("DO0", sc.v("DO0")), ("DO1", sc.v("DO1"))])

        def make_body(eng_name):
            steps = sc.steps[eng_name]

            def body(eng):
                waited = {}
                for emit, waits, inc, inc_n in steps:
                    for s, val in waits:
                        if waited.get(s, 0) < val:
                            eng.wait_ge(sems[s], val)
                            waited[s] = val
                    if emit is not None:
                        ins = emit(eng)
                        if inc is not None:
                            ins.then_inc(sems[inc], inc_n)
            return body

        block.sync(make_body("sync"))
        block.gpsimd(make_body("gpsimd"))
        block.tensor(make_body("tensor"))
        block.vector(make_body("vector"))
        block.scalar(make_body("scalar"))

    return nc


_NC_CACHE = None


def kernel(L0_attn_x, L0_attn_W_qkv, L0_attn_b_qkv, L0_attn_W_o, L0_attn_b_o,
           L0_ffn_w_gate, L0_ffn_w_up, L0_ffn_w_down):
    global _NC_CACHE
    x = np.asarray(L0_attn_x, dtype=np.float32)
    W = np.asarray(L0_attn_W_qkv, dtype=np.float32)
    b = np.asarray(L0_attn_b_qkv, dtype=np.float32)
    Wo = np.asarray(L0_attn_W_o, dtype=np.float32)
    bo = np.asarray(L0_attn_b_o, dtype=np.float32)
    wg = np.asarray(L0_ffn_w_gate, dtype=np.int32)
    wu = np.asarray(L0_ffn_w_up, dtype=np.int32)
    wd = np.asarray(L0_ffn_w_down, dtype=np.int32)

    j = np.arange(3 * D)
    hh = j // (3 * DH)
    rr = j % (3 * DH)
    dd = rr // 3
    ccp = rr % 3
    jp = ccp * D + hh * DH + dd
    W2 = np.empty_like(W)
    W2[jp] = W
    b2 = np.empty_like(b)
    b2[jp] = b

    w2t = np.ascontiguousarray(W2.T)
    wot = np.ascontiguousarray(Wo.T)
    wgt = np.ascontiguousarray(wg.T)
    wut = np.ascontiguousarray(wu.T)
    wdt = np.ascontiguousarray(wd.T)
    ones = np.ones((1, 512), np.float32)
    onescol = np.ones((128, 1), np.float32)
    zrow = np.zeros((1, 128), np.float32)
    ii, jj2 = np.indices((128, 128))
    mask = np.where(ii > jj2, NEG, 0.0).astype(np.float32)
    ccols = np.tile(np.array([[-SHIFT, QSCALE]], np.float32), (128, 1))

    if _NC_CACHE is None:
        _NC_CACHE = build_nc()
    nc = _NC_CACHE

    in_maps = []
    for bidx in range(8):
        in_maps.append({
            "xT": np.ascontiguousarray(x[bidx].T),
            "w2t": w2t,
            "b2": b2.reshape(1, -1),
            "wot": wot,
            "bo": bo.reshape(1, -1),
            "wgt": wgt,
            "wut": wut,
            "wdt": wdt,
            "ones": ones,
            "onescol": onescol,
            "zrow": zrow,
            "mask": mask,
            "ccols": ccols,
        })
    res = run_bass_kernel_spmd(nc, in_maps, core_ids=list(range(8)))
    out = np.stack([np.ascontiguousarray(r["outT"].T) for r in res.results])
    return out.astype(np.float32)


# version 03 (final)
# speedup vs baseline: 1.1670x; 1.1670x over previous
import os
os.environ.setdefault("JAX_PLATFORMS", "cpu")
import numpy as np
import concourse.bass as bass
import concourse.mybir as mybir
from concourse.bass_utils import run_bass_kernel_spmd

F32 = mybir.dt.float32
BF16 = mybir.dt.bfloat16
I32 = mybir.dt.int32
AF = mybir.ActivationFunctionType
ALU = mybir.AluOpType

S = 2048
D = 1024
H = 16
DH = 64
HID = 3072
NCH = D // 128
NT = S // 128
NTB = S // 512
NJC = HID // 128
EPS_LN = 1e-5
EPS_RMS = 1.1920929e-07
QSCALE = 1.0 / 127.0
SHIFT = 20.0
NEG = -1.0e30

ENGINES = ("sync", "gpsimd", "tensor", "vector", "scalar")
DBG = False


class Script:
    def __init__(self):
        self.steps = {e: [] for e in ENGINES}
        self.cnt = {"DB": 0, "DQ": 0, "DWO": 0, "DX0": 0, "DX1": 0,
                    "DF0": 0, "DF1": 0, "DD0": 0, "DD1": 0,
                    "DO0": 0, "DO1": 0, "DT": 0,
                    "PE": 0, "VE": 0, "AC": 0}

    def v(self, name):
        return self.cnt[name]

    def add(self, eng, emit, waits=(), inc=None, inc_n=1):
        w = [(s, val) for (s, val) in waits if val > 0]
        self.steps[eng].append((emit, w, inc, inc_n))
        if inc is not None:
            self.cnt[inc] += inc_n


def build_nc():
    nc = bass.Bass("TRN2", target_bir_lowering=False, debug=False)

    xT = nc.dram_tensor("xT", [D, S], F32, kind="ExternalInput").ap()
    w2t = nc.dram_tensor("w2t", [D, 3 * D], F32, kind="ExternalInput").ap()
    b2 = nc.dram_tensor("b2", [1, 3 * D], F32, kind="ExternalInput").ap()
    wot = nc.dram_tensor("wot", [D, D], F32, kind="ExternalInput").ap()
    bo_d = nc.dram_tensor("bo", [1, D], F32, kind="ExternalInput").ap()
    wgt = nc.dram_tensor("wgt", [D, HID], I32, kind="ExternalInput").ap()
    wut = nc.dram_tensor("wut", [D, HID], I32, kind="ExternalInput").ap()
    wdt = nc.dram_tensor("wdt", [HID, D], I32, kind="ExternalInput").ap()
    ones_d = nc.dram_tensor("ones", [1, 512], F32, kind="ExternalInput").ap()
    onescol_d = nc.dram_tensor("onescol", [128, 1], F32, kind="ExternalInput").ap()
    zrow_d = nc.dram_tensor("zrow", [1, 128], F32, kind="ExternalInput").ap()
    mask_d = nc.dram_tensor("mask", [128, 128], F32, kind="ExternalInput").ap()
    ccols_d = nc.dram_tensor("ccols", [128, 2], F32, kind="ExternalInput").ap()
    outT = nc.dram_tensor("outT", [D, S], F32, kind="ExternalOutput").ap()
    taps = {}
    if DBG:
        for nm, shp in (("tap_rows", [1, 6144]), ("tap_normed", [128, 16384]),
                        ("tap_qt", [128, 4096]), ("tap_vt", [128, 2048]),
                        ("tap_z", [128, 2048]), ("tap_s", [128, 16384]),
                        ("tap_xn", [128, 4096]), ("tap_h", [128, 12288]),
                        ("tap_phat", [128, 2048]), ("tap_zp", [128, 8]),
                        ("tap_xn_all", [128, 16384]), ("tap_h_all", [128, 49152]),
                        ("tap_g_all", [128, 16384])):
            taps[nm] = nc.dram_tensor(nm, shp, F32, kind="ExternalOutput").ap()

    sc = Script()

    from contextlib import ExitStack
    with ExitStack() as stack:
        ent = stack.enter_context
        R1h = ent(nc.sbuf_tensor("R1", [128, 16384], F32))
        R2h = ent(nc.sbuf_tensor("R2", [128, 16384], F32))
        R3h = ent(nc.sbuf_tensor("R3", [128, 4096], F32))
        R4h = ent(nc.sbuf_tensor("R4", [128, 2048], F32))
        R5h = ent(nc.sbuf_tensor("R5", [128, 2048], F32))
        R6h = ent(nc.sbuf_tensor("R6", [128, 4096], F32))
        row_m_h = ent(nc.sbuf_tensor("row_m", [1, 2048], F32))
        row_istd_h = ent(nc.sbuf_tensor("row_istd", [1, 2048], F32))
        row_c2_h = ent(nc.sbuf_tensor("row_c2", [1, 2048], F32))
        bo_sb_h = ent(nc.sbuf_tensor("bo_sb", [1, 1024], F32))
        zparts_h = ent(nc.sbuf_tensor("zparts", [128, 4], F32))
        zcol_h = ent(nc.sbuf_tensor("zcol", [128, 2], F32))
        mcol_h = ent(nc.sbuf_tensor("mcol", [128, 8], F32))
        ones_sb_h = ent(nc.sbuf_tensor("ones_sb", [1, 512], F32))
        onescol_sb_h = ent(nc.sbuf_tensor("onescol_sb", [128, 1], F32))
        zrow_sb_h = ent(nc.sbuf_tensor("zrow_sb", [1, 128], F32))
        ccols_h = ent(nc.sbuf_tensor("ccols_sb", [128, 2], F32))
        pool_h = ent(nc.psum_tensor("pool", [128, 4096], F32))
        semmap = {}
        for nm in ("DB", "DQ", "DWO", "DX0", "DX1", "DF0", "DF1",
                   "DD0", "DD1", "DO0", "DO1", "DT", "PE", "VE", "AC"):
            semmap[nm] = ent(nc.semaphore("s" + nm))
        block = ent(nc.Block())
        R1, R2, R3 = R1h[:, :], R2h[:, :], R3h[:, :]
        R4, R5, R6 = R4h[:, :], R5h[:, :], R6h[:, :]
        row_m, row_istd, row_c2 = row_m_h[:, :], row_istd_h[:, :], row_c2_h[:, :]
        bo_sb = bo_sb_h[:, :]
        zparts, zcol = zparts_h[:, :], zcol_h[:, :]
        mcol = mcol_h[:, :]
        ones_sb = ones_sb_h[:, :]
        onescol, zrow_sb = onescol_sb_h[:, :], zrow_sb_h[:, :]
        negshift = ccols_h[:, 0:1]
        qscale_col = ccols_h[:, 1:2]
        mask_sb = R6[:, 3200:3328]
        bias_buf = R6[0:1, 3328:3840]
        R1b = R1.bitcast(BF16)
        R6b = R6.bitcast(BF16)
        R3b = R3.bitcast(BF16)
        P = pool_h[:, :]
        sems = semmap

        def bank(b, w=512):
            return P[:, b * 512: b * 512 + w]

        sc.add("sync", lambda e: e.dma_start(out=ones_sb, in_=ones_d), inc="DB", inc_n=16)
        sc.add("sync", lambda e: e.dma_start(out=onescol, in_=onescol_d), inc="DB", inc_n=16)
        sc.add("sync", lambda e: e.dma_start(out=zrow_sb, in_=zrow_d), inc="DB", inc_n=16)
        sc.add("sync", lambda e: e.dma_start(out=mask_sb, in_=mask_d), inc="DB", inc_n=16)
        sc.add("sync", lambda e: e.dma_start(out=bo_sb, in_=bo_d), inc="DB", inc_n=16)
        sc.add("sync", lambda e: e.dma_start(out=ccols_h[:, :], in_=ccols_d), inc="DB", inc_n=16)
        sc.add("sync",
               lambda e: e.dma_start(out=R2.rearrange("p (c t) -> p c t", c=NCH),
                                     in_=xT.rearrange("(c p) t -> p c t", p=128)),
               inc="DB", inc_n=16)
        dw_boot = sc.v("DB")

        for tb in range(NTB):
            for c in range(NCH):
                last = (tb == NTB - 1 and c == NCH - 1)
                sc.add("tensor",
                       lambda e, tb=tb, c=c: e.matmul(
                           P[0:1, tb * 512:(tb + 1) * 512],
                           onescol,
                           R2[:, c * 2048 + tb * 512: c * 2048 + (tb + 1) * 512],
                           start=(c == 0), stop=(c == NCH - 1)),
                       waits=[("DB", dw_boot)] if (tb == 0 and c == 0) else (),
                       inc="PE" if last else None)
        pe_sum = sc.v("PE")

        sq_slot_pe = {}
        for tb in range(NTB):
            for c in range(NCH):
                slot = (tb * NCH + c) % 4
                sc.add("scalar",
                       lambda e, c=c, tb=tb, slot=slot: e.activation(
                           R4[:, slot * 512:(slot + 1) * 512],
                           R2[:, c * 2048 + tb * 512: c * 2048 + (tb + 1) * 512],
                           AF.Square),
                       waits=([("DB", dw_boot)] if (c == 0 and tb == 0) else [])
                       + [("PE", sq_slot_pe.get(slot, 0))],
                       inc="AC")
                ac_sq = sc.v("AC")
                sc.add("tensor",
                       lambda e, c=c, tb=tb, slot=slot: e.matmul(
                           P[0:1, 2048 + tb * 512: 2048 + (tb + 1) * 512],
                           onescol,
                           R4[:, slot * 512:(slot + 1) * 512],
                           start=(c == 0), stop=(c == NCH - 1)),
                       waits=[("AC", ac_sq)],
                       inc="PE")
                sq_slot_pe[slot] = sc.v("PE")
        pe_sumsq = sc.v("PE")

        rowC = R5[0:1, 0:2048]
        rowD = R4[0:1, 0:2048]

        def vadd(emit, waits=()):
            sc.add("vector", emit, list(waits) + [("VE", sc.v("VE"))], inc="VE")

        vadd(lambda e: e.tensor_copy(row_m, P[0:1, 0:2048]), [("PE", pe_sum)])
        vadd(lambda e: e.tensor_scalar_mul(row_m, row_m, 1.0 / D))
        vadd(lambda e: e.tensor_copy(rowD, P[0:1, 2048:4096]), [("PE", pe_sumsq)])
        vadd(lambda e: e.tensor_scalar_mul(rowD, rowD, 1.0 / D))
        vadd(lambda e: e.tensor_mul(rowC, row_m, row_m))
        vadd(lambda e: e.tensor_sub(rowD, rowD, rowC))
        vadd(lambda e: e.tensor_scalar_add(rowC, rowD, EPS_LN))
        sc.add("scalar", lambda e: e.activation(row_istd, rowC, AF.Sqrt),
               [("VE", sc.v("VE"))], inc="AC")
        vadd(lambda e: e.reciprocal(row_c2, row_istd), [("AC", sc.v("AC"))])
        vadd(lambda e: e.tensor_mul(row_istd, rowC, row_c2))
        vadd(lambda e: e.tensor_mul(row_istd, row_istd, row_c2))
        vadd(lambda e: e.tensor_scalar(row_istd, row_istd, -0.5, 1.5,
                                       ALU.mult, ALU.add))
        vadd(lambda e: e.tensor_mul(row_istd, row_c2, row_istd))
        vadd(lambda e: e.reciprocal(row_c2, rowC))
        vadd(lambda e: e.tensor_mul(rowD, rowD, row_c2))
        vadd(lambda e: e.tensor_scalar_add(rowD, rowD, EPS_RMS))
        sc.add("scalar", lambda e: e.activation(rowC, rowD, AF.Sqrt),
               [("VE", sc.v("VE"))], inc="AC")
        vadd(lambda e: e.reciprocal(row_c2, rowC), [("AC", sc.v("AC"))])
        vadd(lambda e: e.tensor_mul(rowC, rowD, row_c2))
        vadd(lambda e: e.tensor_mul(rowC, rowC, row_c2))
        vadd(lambda e: e.tensor_scalar(rowC, rowC, -0.5, 1.5, ALU.mult, ALU.add))
        vadd(lambda e: e.tensor_mul(row_c2, row_c2, rowC))
        vadd(lambda e: e.tensor_mul(row_c2, row_c2, row_istd))
        ve_rows = sc.v("VE")

        ve_prev_tb = 0
        for tb in range(NTB):
            sc.add("tensor",
                   lambda e, tb=tb: e.matmul(
                       bank(4), ones_sb[0:1, 0:128],
                       row_m[0:1, tb * 512:(tb + 1) * 512], start=True, stop=True),
                   waits=[("VE", ve_rows), ("VE", ve_prev_tb)])
            sc.add("tensor",
                   lambda e, tb=tb: e.matmul(
                       bank(5), ones_sb[0:1, 0:128],
                       row_c2[0:1, tb * 512:(tb + 1) * 512], start=True, stop=True),
                   inc="PE")
            pe_bc = sc.v("PE")
            for c in range(NCH):
                sc.add("vector",
                       lambda e, tb=tb, c=c: e.tensor_sub(
                           R1[:, c * 2048 + tb * 512: c * 2048 + (tb + 1) * 512],
                           R2[:, c * 2048 + tb * 512: c * 2048 + (tb + 1) * 512],
                           bank(4)),
                       waits=[("PE", pe_bc)], inc="VE")
                sc.add("vector",
                       lambda e, tb=tb, c=c: e.tensor_mul(
                           R1[:, c * 2048 + tb * 512: c * 2048 + (tb + 1) * 512],
                           R1[:, c * 2048 + tb * 512: c * 2048 + (tb + 1) * 512],
                           bank(5)),
                       waits=[("VE", sc.v("VE"))], inc="VE")
            ve_prev_tb = sc.v("VE")
        ve_normed = sc.v("VE")
        if DBG:
            sc.add("sync", lambda e: e.dma_start(out=taps["tap_rows"][0:1, 0:2048], in_=row_m),
                   waits=[("VE", ve_rows)], inc="DT", inc_n=16)
            sc.add("sync", lambda e: e.dma_start(out=taps["tap_rows"][0:1, 2048:4096], in_=row_istd),
                   inc="DT", inc_n=16)
            sc.add("sync", lambda e: e.dma_start(out=taps["tap_rows"][0:1, 4096:6144], in_=row_c2),
                   inc="DT", inc_n=16)
            sc.add("sync", lambda e: e.dma_start(out=taps["tap_normed"], in_=R1),
                   waits=[("VE", ve_normed)], inc="DT", inc_n=16)

        pe_end_pair = 0
        ve_z_evac = 0
        ac_bank01 = 0
        ac_bank23 = 0
        ve_s_acc = {0: 0, 1: 0}
        qk_bank_free = {0: 0, 1: 0, 2: 0, 3: 0}
        pe_scores_end = 0
        pe_attn_all = 0

        dbg_gate = {}
        for p in range(8):
            sc.add("sync",
                   lambda e, p=p: e.dma_start(
                       out=R6[:, 0:1024].rearrange("p (c j) -> p c j", j=128),
                       in_=w2t[:, p * 128:(p + 1) * 128].rearrange("(c p) j -> p c j", p=128)),
                   waits=[("PE", pe_end_pair), ("AC", ac_bank01)],
                   inc="DQ", inc_n=16)
            sc.add("sync",
                   lambda e, p=p: e.dma_start(
                       out=R6[:, 1024:2048].rearrange("p (c j) -> p c j", j=128),
                       in_=w2t[:, 1024 + p * 128: 1024 + (p + 1) * 128]
                       .rearrange("(c p) j -> p c j", p=128)),
                   inc="DQ", inc_n=16)
            sc.add("sync",
                   lambda e, p=p: e.dma_start(
                       out=R6[:, 2048:3072].rearrange("p (c j) -> p c j", j=128),
                       in_=w2t[:, 2048 + p * 128: 2048 + (p + 1) * 128]
                       .rearrange("(c p) j -> p c j", p=128)),
                   inc="DQ", inc_n=16)
            for ci in range(3):
                sc.add("sync",
                       lambda e, p=p, ci=ci: e.dma_start(
                           out=bias_buf[0:1, ci * 128:(ci + 1) * 128],
                           in_=b2[0:1, ci * 1024 + p * 128: ci * 1024 + (p + 1) * 128]),
                       inc="DQ", inc_n=16)
            dw_qkvw = sc.v("DQ")

            pe_q_done = 0
            for which, woff, boff, r3off in (("q", 0, 0, 0), ("k", 1024, 128, 2048)):
                for tb in range(NTB):
                    bq = tb % 2
                    for c in range(NCH):
                        sc.add("tensor",
                               lambda e, woff=woff, c=c, tb=tb, bq=bq: e.matmul(
                                   bank(bq), R6[:, woff + c * 128: woff + (c + 1) * 128],
                                   R1[:, c * 2048 + tb * 512: c * 2048 + (tb + 1) * 512],
                                   start=(c == 0), stop=False),
                               waits=([("DQ", dw_qkvw), ("VE", ve_normed),
                                       ("VE", qk_bank_free[bq])] if c == 0 else ()))
                    sc.add("tensor",
                           lambda e, boff=boff, bq=bq: e.matmul(
                               bank(bq), bias_buf[0:1, boff:boff + 128],
                               ones_sb[0:1, 0:512], start=False, stop=True),
                           inc="PE")
                    pe_proj = sc.v("PE")
                    sc.add("vector",
                           lambda e, r3off=r3off, tb=tb, bq=bq: e.tensor_copy(
                               R3[:, r3off + tb * 512: r3off + (tb + 1) * 512], bank(bq)),
                           waits=[("PE", pe_proj)], inc="VE")
                    qk_bank_free[bq] = sc.v("VE")
                if which == "q":
                    pe_q_done = sc.v("PE")
            sc.add("sync",
                   lambda e, p=p: e.dma_start(out=R6[:, 0:1024],
                                              in_=wot[p * 128:(p + 1) * 128, :]),
                   waits=[("PE", pe_q_done)], inc="DWO", inc_n=16)
            dw_wo = sc.v("DWO")

            for tt in range(NT):
                bv = 2 + tt % 2
                for c in range(NCH):
                    sc.add("tensor",
                           lambda e, c=c, tt=tt, bv=bv: e.matmul(
                               bank(bv, 128),
                               R1[:, c * 2048 + tt * 128: c * 2048 + (tt + 1) * 128],
                               R6[:, 2048 + c * 128: 2048 + (c + 1) * 128],
                               start=(c == 0), stop=False),
                           waits=[("VE", qk_bank_free[bv])] if c == 0 else ())
                sc.add("tensor",
                       lambda e, bv=bv: e.matmul(
                           bank(bv, 128), ones_sb[0:1, 0:128],
                           bias_buf[0:1, 256:384], start=False, stop=True),
                       inc="PE")
                pe_v = sc.v("PE")
                sc.add("vector",
                       lambda e, tt=tt, bv=bv: e.tensor_copy(
                           R4[:, tt * 128:(tt + 1) * 128], bank(bv, 128)),
                       waits=[("PE", pe_v)], inc="VE")
                qk_bank_free[bv] = sc.v("VE")

            if DBG and p == 0:
                sc.add("sync", lambda e: e.dma_start(out=taps["tap_qt"], in_=R3),
                       waits=[("VE", sc.v("VE"))], inc="DT", inc_n=16)
                sc.add("sync", lambda e: e.dma_start(out=taps["tap_vt"], in_=R4),
                       inc="DT", inc_n=16)
            for qb in range(4):
                sc.add("tensor",
                       lambda e, qb=qb: e.matmul(
                           bank(4 + qb), zrow_sb, ones_sb[0:1, 0:512],
                           start=True, stop=True),
                       waits=([("VE", ve_z_evac)] if qb == 0 else ()),
                       inc="PE" if qb == 3 else None)

            pe_avA = sc.v("PE")
            pe_avB = sc.v("PE")
            ve_vsA_prev = 0
            ve_vsB_prev = 0
            ac_mcol6 = 0
            for kc in range(NT):
                k0 = kc * 128
                halves = [(k0 + sh * 1024, min(S, k0 + (sh + 1) * 1024))
                          for sh in range(2) if k0 + sh * 1024 < S]
                nh = len(halves)
                pe_sc = {}
                ac_exp = {}
                for hx, rows, xref in (("A", (0, 64), 0), ("B", (64, 128), 1)):
                    nblk = 0
                    for sh, (qlo, qhi) in enumerate(halves):
                        coff = sh * 1024
                        first_waits = []
                        if hx == "A":
                            first_waits.append(
                                ("AC", ac_bank01 if sh == 0 else ac_bank23))
                        else:
                            first_waits.append(("AC", ac_exp[("A", sh)]))
                        qs = qlo
                        while qs < qhi:
                            w = min(512, qhi - qs)
                            lastmm = (qs + w >= qhi)
                            sc.add("tensor",
                                   lambda e, rows=rows, coff=coff, qs=qs, w=w,
                                   qlo=qlo, k0=k0: e.matmul(
                                       P[:, coff + (qs - qlo): coff + (qs - qlo) + w],
                                       R3[rows[0]:rows[1], 2048 + k0: 2048 + k0 + 128],
                                       R3[rows[0]:rows[1], qs: qs + w],
                                       start=True, stop=True),
                                   waits=first_waits if qs == qlo else (),
                                   inc="PE" if lastmm else None)
                            qs += w
                        pe_sc[(hx, sh)] = sc.v("PE")
                        if sh == 0:
                            sc.add("vector",
                                   lambda e, coff=coff: e.tensor_add(
                                       P[:, coff:coff + 128],
                                       P[:, coff:coff + 128], mask_sb),
                                   waits=[("PE", pe_sc[(hx, sh)])], inc="VE")
                        qs = qlo
                        while qs < qhi:
                            w = min(512, qhi - qs)
                            sc.add("vector",
                                   lambda e, coff=coff, qs=qs, w=w, qlo=qlo,
                                   nblk=nblk: e.reduce_max(
                                       mcol[:, nblk:nblk + 1],
                                       P[:, coff + (qs - qlo): coff + (qs - qlo) + w],
                                       axis=mybir.AxisListType.X),
                                   waits=[("PE", pe_sc[(hx, sh)])], inc="VE")
                            nblk += 1
                            qs += w
                    sc.add("vector",
                           lambda e, nblk=nblk: e.reduce_max(
                               mcol[:, 6:7], mcol[:, 0:nblk],
                               axis=mybir.AxisListType.X, negate=True),
                           waits=[("AC", ac_mcol6), ("VE", sc.v("VE"))],
                           inc="VE")
                    ve_negm = sc.v("VE")
                    for sh, (qlo, qhi) in enumerate(halves):
                        coff = sh * 1024
                        zp = (0 if hx == "A" else 2) + sh
                        ewaits = [("VE", ve_negm),
                                  ("PE", pe_avA if hx == "B" else pe_avB),
                                  ("VE", ve_vsA_prev if hx == "A" else ve_vsB_prev),
                                  ("DT", dbg_gate.get("phat", 0))]
                        sc.add("scalar",
                               lambda e, coff=coff, qlo=qlo, qhi=qhi, zp=zp: e.activation(
                                   R5[:, qlo:qhi], P[:, coff:coff + (qhi - qlo)],
                                   AF.Exp, bias=mcol[:, 6:7],
                                   accum_out=zparts[:, zp:zp + 1]),
                               waits=ewaits, inc="AC")
                        ac_exp[(hx, sh)] = sc.v("AC")
                        if DBG and p == 0 and kc == 0 and hx == "A" and sh == nh - 1:
                            sc.add("sync", lambda e: e.dma_start(out=taps["tap_phat"], in_=R5),
                                   waits=[("AC", sc.v("AC"))], inc="DT", inc_n=16)
                            dbg_gate["phat"] = sc.v("DT")
                        if sh == 0:
                            ac_bank01 = sc.v("AC")
                        else:
                            ac_bank23 = sc.v("AC")
                    ac_mcol6 = sc.v("AC")
                    zlo = 0 if hx == "A" else 2
                    voff = 3072 if hx == "A" else 3136
                    vcol = 0 if hx == "A" else 64
                    if nh == 2:
                        sc.add("vector",
                               lambda e, zlo=zlo: e.tensor_add(
                                   zcol[:, 0:1], zparts[:, zlo:zlo + 1],
                                   zparts[:, zlo + 1:zlo + 2]),
                               waits=[("AC", ac_exp[(hx, nh - 1)]),
                                      ("DT", dbg_gate.get("zp", 0))], inc="VE")
                    else:
                        sc.add("vector",
                               lambda e, zlo=zlo: e.tensor_copy(
                                   zcol[:, 0:1], zparts[:, zlo:zlo + 1]),
                               waits=[("AC", ac_exp[(hx, nh - 1)])], inc="VE")
                    sc.add("vector", lambda e: e.reciprocal(zcol[:, 1:2], zcol[:, 0:1]),
                           waits=[("VE", sc.v("VE"))], inc="VE")
                    sc.add("vector",
                           lambda e, voff=voff, k0=k0, vcol=vcol: e.tensor_scalar_mul(
                               R6[:, voff:voff + 64],
                               R4[:, k0 + vcol: k0 + vcol + 64],
                               zcol[:, 1:2]),
                           waits=[("PE", pe_avA if hx == "A" else pe_avB),
                                  ("VE", sc.v("VE"))],
                           inc="VE")
                    ve_vs = sc.v("VE")
                    if DBG and p == 0 and kc == 0 and hx == "A":
                        sc.add("sync", lambda e: e.dma_start(
                                   out=taps["tap_zp"][:, 0:4], in_=zparts),
                               waits=[("VE", ve_vs)], inc="DT", inc_n=16)
                        sc.add("sync", lambda e: e.dma_start(
                                   out=taps["tap_zp"][:, 4:6], in_=zcol),
                               inc="DT", inc_n=16)
                        dbg_gate["zp"] = sc.v("DT")
                    if hx == "A":
                        ve_vsA_prev = ve_vs
                    else:
                        ve_vsB_prev = ve_vs
                    prange = (0, 64) if hx == "A" else (64, 128)
                    tp = None if hx == "A" else (0, 64)
                    qbs = list(range(k0 // 512, 4))
                    for i, qb in enumerate(qbs):
                        qs = max(qb * 512, k0)
                        w = (qb + 1) * 512 - qs
                        sc.add("tensor",
                               lambda e, prange=prange, qs=qs, w=w, voff=voff, tp=tp:
                               e.matmul(
                                   P[prange[0]:prange[1], 2048 + qs: 2048 + qs + w],
                                   R6[:, voff:voff + 64],
                                   R5[:, qs:qs + w],
                                   start=False, stop=False, tile_position=tp),
                               waits=([("VE", ve_vs), ("AC", ac_exp[(hx, nh - 1)])]
                                      if i == 0 else ()),
                               inc="PE" if i == len(qbs) - 1 else None)
                    if hx == "A":
                        pe_avA = sc.v("PE")
                    else:
                        pe_avB = sc.v("PE")
            pe_scores_end = sc.v("PE")

            sc.add("vector",
                   lambda e: e.tensor_copy(R4[:, 0:2048], P[:, 2048:4096]),
                   waits=[("PE", pe_avB)], inc="VE")
            ve_z_evac = sc.v("VE")
            if DBG and p == 0:
                sc.add("sync", lambda e: e.dma_start(out=taps["tap_z"], in_=R4),
                       waits=[("VE", ve_z_evac)], inc="DT", inc_n=16)
            for dc in range(NCH):
                for tb in range(NTB):
                    bb = (dc * NTB + tb) % 2
                    if p == 0:
                        sc.add("tensor",
                               lambda e, dc=dc, tb=tb, bb=bb: e.matmul(
                                   bank(bb), R6[:, dc * 128:(dc + 1) * 128],
                                   R4[:, tb * 512:(tb + 1) * 512],
                                   start=True, stop=False),
                               waits=[("VE", ve_z_evac), ("DWO", dw_wo),
                                      ("VE", ve_s_acc[bb]), ("AC", ac_bank01),
                                      ("AC", ac_bank23)])
                        sc.add("tensor",
                               lambda e, dc=dc, bb=bb: e.matmul(
                                   bank(bb), bo_sb[0:1, dc * 128:(dc + 1) * 128],
                                   ones_sb[0:1, 0:512], start=False, stop=True),
                               inc="PE")
                    else:
                        sc.add("tensor",
                               lambda e, dc=dc, tb=tb, bb=bb: e.matmul(
                                   bank(bb), R6[:, dc * 128:(dc + 1) * 128],
                                   R4[:, tb * 512:(tb + 1) * 512],
                                   start=True, stop=True),
                               waits=[("VE", ve_z_evac), ("DWO", dw_wo),
                                      ("VE", ve_s_acc[bb]), ("AC", ac_bank01),
                                      ("AC", ac_bank23)],
                               inc="PE")
                    pe_wo = sc.v("PE")
                    if p == 0:
                        sc.add("vector",
                               lambda e, dc=dc, tb=tb, bb=bb: e.tensor_copy(
                                   R2[:, dc * 2048 + tb * 512: dc * 2048 + (tb + 1) * 512],
                                   bank(bb)),
                               waits=[("PE", pe_wo)], inc="VE")
                    else:
                        sc.add("vector",
                               lambda e, dc=dc, tb=tb, bb=bb: e.tensor_add(
                                   R2[:, dc * 2048 + tb * 512: dc * 2048 + (tb + 1) * 512],
                                   R2[:, dc * 2048 + tb * 512: dc * 2048 + (tb + 1) * 512],
                                   bank(bb)),
                               waits=[("PE", pe_wo)], inc="VE")
                    ve_s_acc[bb] = sc.v("VE")
            pe_end_pair = sc.v("PE")
        pe_attn_all = sc.v("PE")
        if DBG:
            sc.add("sync", lambda e: e.dma_start(out=taps["tap_s"], in_=R2),
                   waits=[("VE", sc.v("VE"))], inc="DT", inc_n=16)

        ffn_slot_free = {0: pe_attn_all, 1: pe_attn_all}
        wd_free = {0: pe_attn_all, 1: pe_scores_end}
        r3slot_ve = {0: 0, 1: 0}
        ve_xn_done = 0
        pe_xn_readers = 0
        h_done = {}
        ve_comb = {}
        do_out = {}
        ac_g_prev = 0
        oidx = 0
        for tb in range(NTB):
            sc.add("tensor",
                   lambda e, tb=tb: e.matmul(
                       bank(6), ones_sb[0:1, 0:128],
                       row_m[0:1, tb * 512:(tb + 1) * 512], start=True, stop=True),
                   waits=[("VE", ve_z_evac), ("VE", ve_xn_done)])
            sc.add("tensor",
                   lambda e, tb=tb: e.matmul(
                       bank(7), ones_sb[0:1, 0:128],
                       row_istd[0:1, tb * 512:(tb + 1) * 512], start=True, stop=True),
                   inc="PE")
            pe_bc4 = sc.v("PE")
            for c in range(NCH):
                slot = c % 2
                sc.add("sync",
                       lambda e, c=c, tb=tb, slot=slot: e.dma_start(
                           out=R3[:, slot * 512:(slot + 1) * 512],
                           in_=xT[c * 128:(c + 1) * 128, tb * 512:(tb + 1) * 512]),
                       waits=[("PE", pe_scores_end), ("VE", r3slot_ve[slot])],
                       inc="DX%d" % slot, inc_n=16)
                dw_x2 = ("DX%d" % slot, sc.v("DX%d" % slot))
                sc.add("vector",
                       lambda e, c=c, slot=slot: e.tensor_sub(
                           R4[:, 0:512],
                           R3[:, slot * 512:(slot + 1) * 512], bank(6)),
                       waits=[dw_x2, ("PE", pe_bc4), ("PE", pe_attn_all),
                              ("PE", pe_xn_readers), ("DT", dbg_gate.get("xn", 0)),
                              ("VE", sc.v("VE"))],
                       inc="VE")
                r3slot_ve[slot] = sc.v("VE")
                sc.add("vector",
                       lambda e: e.tensor_mul(R4[:, 512:1024], R4[:, 0:512], bank(7)),
                       waits=[("VE", sc.v("VE"))], inc="VE")
                sc.add("vector",
                       lambda e, c=c: e.tensor_copy(
                           R1b[:, 24576 + c * 512: 24576 + (c + 1) * 512],
                           R4[:, 512:1024]),
                       waits=[("VE", sc.v("VE"))], inc="VE")
                sc.add("vector",
                       lambda e, c=c: e.tensor_sub(
                           R1b[:, 28672 + c * 512: 28672 + (c + 1) * 512],
                           R4[:, 512:1024],
                           R1b[:, 24576 + c * 512: 24576 + (c + 1) * 512]),
                       waits=[("VE", sc.v("VE"))], inc="VE")
            ve_xn_done = sc.v("VE")
            if DBG:
                sc.add("sync", lambda e, tb=tb: e.dma_start(
                           out=taps["tap_xn_all"][:, tb * 4096:(tb + 1) * 4096],
                           in_=R1[:, 12288:16384]),
                       waits=[("VE", ve_xn_done)], inc="DT", inc_n=16)
                dbg_gate["xn"] = sc.v("DT")
            if DBG and tb == 0:
                sc.add("sync", lambda e: e.dma_start(out=taps["tap_xn"], in_=R1[:, 12288:16384]),
                       waits=[("VE", ve_xn_done)], inc="DT", inc_n=16)

            for jb in range(NJC):
                slot = jb % 2
                soff = 2048 * slot
                sc.add("gpsimd",
                       lambda e, jb=jb, soff=soff: e.dma_start(
                           out=R6b[:, soff:soff + 1024].rearrange("p (c j) -> p c j", j=128),
                           in_=wgt[:, jb * 128:(jb + 1) * 128]
                           .rearrange("(c p) j -> p c j", p=128)),
                       waits=[("PE", ffn_slot_free[slot])],
                       inc="DF%d" % slot, inc_n=16)
                sc.add("gpsimd",
                       lambda e, jb=jb, soff=soff: e.dma_start(
                           out=R6b[:, soff + 1024:soff + 2048]
                           .rearrange("p (c j) -> p c j", j=128),
                           in_=wut[:, jb * 128:(jb + 1) * 128]
                           .rearrange("(c p) j -> p c j", p=128)),
                       inc="DF%d" % slot, inc_n=16)
                dg_w = ("DF%d" % slot, sc.v("DF%d" % slot))
                for hl in range(2):
                    xoff = 24576 if hl == 0 else 28672
                    for c in range(NCH):
                        sc.add("tensor",
                               lambda e, c=c, soff=soff, slot=slot, xoff=xoff, hl=hl: e.matmul(
                                   bank(slot), R6b[:, soff + c * 128: soff + (c + 1) * 128],
                                   R1b[:, xoff + c * 512: xoff + (c + 1) * 512],
                                   start=(hl == 0 and c == 0),
                                   stop=(hl == 1 and c == NCH - 1)),
                               waits=([dg_w, ("VE", ve_xn_done),
                                       ("AC", ac_g_prev if jb < 2 else 0),
                                       ("VE", h_done.get(jb - 2, 0))]
                                      if (hl == 0 and c == 0) else ()),
                               inc="PE" if (hl == 1 and c == NCH - 1) else None)
                pe_gate = sc.v("PE")
                for hl in range(2):
                    xoff = 24576 if hl == 0 else 28672
                    for c in range(NCH):
                        sc.add("tensor",
                               lambda e, c=c, soff=soff, slot=slot, xoff=xoff, hl=hl: e.matmul(
                                   bank(2 + slot),
                                   R6b[:, soff + 1024 + c * 128: soff + 1024 + (c + 1) * 128],
                                   R1b[:, xoff + c * 512: xoff + (c + 1) * 512],
                                   start=(hl == 0 and c == 0),
                                   stop=(hl == 1 and c == NCH - 1)),
                               inc="PE" if (hl == 1 and c == NCH - 1) else None)
                pe_up = sc.v("PE")
                pe_xn_readers = pe_up
                ffn_slot_free[slot] = pe_up
                sc.add("scalar",
                       lambda e, slot=slot: e.activation(
                           R5[:, 1024:1536], bank(slot), AF.Silu, scale=qscale_col),
                       waits=[("PE", pe_gate), ("VE", h_done.get(jb - 1, 0))],
                       inc="AC")
                ac_silu = sc.v("AC")
                sc.add("vector",
                       lambda e, jb=jb, slot=slot: e.scalar_tensor_tensor(
                           R4[:, 1024:1536], bank(2 + slot), QSCALE,
                           R5[:, 1024:1536], ALU.mult, ALU.mult),
                       waits=[("AC", ac_silu), ("PE", pe_up),
                              ("DT", dbg_gate.get("h", 0)),
                              ("VE", sc.v("VE"))], inc="VE")
                sc.add("vector",
                       lambda e, jb=jb: e.tensor_copy(
                           R1b[:, jb * 512:(jb + 1) * 512], R4[:, 1024:1536]),
                       waits=[("VE", sc.v("VE"))], inc="VE")
                sc.add("vector",
                       lambda e, jb=jb: e.tensor_sub(
                           R1b[:, 12288 + jb * 512: 12288 + (jb + 1) * 512],
                           R4[:, 1024:1536],
                           R1b[:, jb * 512:(jb + 1) * 512]),
                       waits=[("VE", sc.v("VE"))], inc="VE")
                h_done[jb] = sc.v("VE")
            ve_h_all = sc.v("VE")
            if DBG:
                sc.add("sync", lambda e, tb=tb: e.dma_start(
                           out=taps["tap_h_all"][:, tb * 12288:(tb + 1) * 12288],
                           in_=R1[:, 0:12288]),
                       waits=[("VE", ve_h_all)], inc="DT", inc_n=16)
                dbg_gate["h"] = sc.v("DT")
            if DBG and tb == 0:
                sc.add("sync", lambda e: e.dma_start(out=taps["tap_h"], in_=R1[:, 0:12288]),
                       waits=[("VE", ve_h_all)], inc="DT", inc_n=16)
            ffn_slot_free = dict(ffn_slot_free)

            for dc in range(NCH):
                wslot = dc % 2
                wd_dst = R6b[:, 4096:7168] if wslot == 0 else R3b[:, 2048:5120]
                base_gate = ([("PE", ffn_slot_free[0]), ("PE", ffn_slot_free[1])]
                             if wslot == 0 else [("VE", ve_xn_done)])
                for half in range(2):
                    sc.add("gpsimd",
                           lambda e, dc=dc, half=half, wd_dst=wd_dst: e.dma_start(
                               out=wd_dst[:, half * 1536:(half + 1) * 1536]
                               .rearrange("p (j d) -> p j d", d=128),
                               in_=wdt[half * 1536:(half + 1) * 1536,
                                       dc * 128:(dc + 1) * 128]
                               .rearrange("(j p) d -> p j d", p=128)),
                           waits=(base_gate + [("PE", wd_free[wslot])])
                           if half == 0 else (),
                           inc="DD%d" % wslot, inc_n=16)
                dg_wd = ("DD%d" % wslot, sc.v("DD%d" % wslot))
                xslot = dc % 2
                sc.add("sync",
                       lambda e, dc=dc, tb=tb, xslot=xslot: e.dma_start(
                           out=R3[:, xslot * 512:(xslot + 1) * 512],
                           in_=xT[dc * 128:(dc + 1) * 128, tb * 512:(tb + 1) * 512]),
                       waits=[("VE", ve_xn_done), ("VE", r3slot_ve[xslot])],
                       inc="DX%d" % xslot, inc_n=16)
                dw_xc = ("DX%d" % xslot, sc.v("DX%d" % xslot))
                for hl in range(2):
                    hoff = 0 if hl == 0 else 12288
                    for jc in range(NJC):
                        sc.add("tensor",
                               lambda e, jc=jc, wd_dst=wd_dst, dc=dc, hoff=hoff, hl=hl: e.matmul(
                                   bank(4 + dc % 2), wd_dst[:, jc * 128:(jc + 1) * 128],
                                   R1b[:, hoff + jc * 512: hoff + (jc + 1) * 512],
                                   start=(hl == 0 and jc == 0),
                                   stop=(hl == 1 and jc == NJC - 1)),
                               waits=([dg_wd, ("VE", ve_h_all),
                                       ("VE", ve_comb.get(oidx - 2, 0))]
                                      if (hl == 0 and jc == 0) else ()),
                               inc="PE" if (hl == 1 and jc == NJC - 1) else None)
                pe_y = sc.v("PE")
                wd_free[wslot] = pe_y
                sc.add("scalar",
                       lambda e, dc=dc: e.activation(
                           R5[:, 1536:2048], bank(4 + dc % 2), AF.Silu, scale=qscale_col),
                       waits=[("PE", pe_y), ("VE", ve_comb.get(oidx - 1, 0))],
                       inc="AC")
                ac_g = sc.v("AC")
                ac_g_prev = ac_g
                if DBG:
                    sc.add("sync", lambda e, tb=tb, dc=dc: e.dma_start(
                               out=taps["tap_g_all"][:, (tb * 8 + dc) * 512:
                                                     (tb * 8 + dc + 1) * 512],
                               in_=R5[:, 1536:2048]),
                           waits=[("AC", ac_g)], inc="DT", inc_n=16)
                    dbg_gate["g"] = sc.v("DT")
                oslot = dc % 2
                sc.add("vector",
                       lambda e, dc=dc, tb=tb, oslot=oslot: e.tensor_mul(
                           R5[:, oslot * 512:(oslot + 1) * 512],
                           R2[:, dc * 2048 + tb * 512: dc * 2048 + (tb + 1) * 512],
                           R5[:, 1536:2048]),
                       waits=[do_out.get(oidx - 2, ("DO0", 0)),
                              ("DT", dbg_gate.get("g", 0)), ("AC", ac_g)],
                       inc="VE")
                sc.add("vector",
                       lambda e, oslot=oslot, xslot=xslot: e.tensor_add(
                           R5[:, oslot * 512:(oslot + 1) * 512],
                           R5[:, oslot * 512:(oslot + 1) * 512],
                           R3[:, xslot * 512:(xslot + 1) * 512]),
                       waits=[dw_xc, ("VE", sc.v("VE"))], inc="VE")
                ve_comb[oidx] = sc.v("VE")
                r3slot_ve[xslot] = sc.v("VE")
                sc.add("sync",
                       lambda e, dc=dc, tb=tb, oslot=oslot: e.dma_start(
                           out=outT[dc * 128:(dc + 1) * 128, tb * 512:(tb + 1) * 512],
                           in_=R5[:, oslot * 512:(oslot + 1) * 512]),
                       waits=[("VE", ve_comb[oidx])], inc="DO%d" % oslot, inc_n=16)
                do_out[oidx] = ("DO%d" % oslot, sc.v("DO%d" % oslot))
                oidx += 1
            ffn_slot_free = {0: sc.v("PE"), 1: sc.v("PE")}
            wd_free = {0: sc.v("PE"), 1: sc.v("PE")}

        sc.add("sync", None, waits=[("DO0", sc.v("DO0")), ("DO1", sc.v("DO1"))])

        def make_body(eng_name):
            steps = sc.steps[eng_name]

            def body(eng):
                waited = {}
                for emit, waits, inc, inc_n in steps:
                    for s, val in waits:
                        if waited.get(s, 0) < val:
                            eng.wait_ge(sems[s], val)
                            waited[s] = val
                    if emit is not None:
                        ins = emit(eng)
                        if inc is not None:
                            ins.then_inc(sems[inc], inc_n)
            return body

        block.sync(make_body("sync"))
        block.gpsimd(make_body("gpsimd"))
        block.tensor(make_body("tensor"))
        block.vector(make_body("vector"))
        block.scalar(make_body("scalar"))

    return nc


_NC_CACHE = None


def kernel(L0_attn_x, L0_attn_W_qkv, L0_attn_b_qkv, L0_attn_W_o, L0_attn_b_o,
           L0_ffn_w_gate, L0_ffn_w_up, L0_ffn_w_down):
    global _NC_CACHE
    x = np.asarray(L0_attn_x, dtype=np.float32)
    W = np.asarray(L0_attn_W_qkv, dtype=np.float32)
    b = np.asarray(L0_attn_b_qkv, dtype=np.float32)
    Wo = np.asarray(L0_attn_W_o, dtype=np.float32)
    bo = np.asarray(L0_attn_b_o, dtype=np.float32)
    wg = np.asarray(L0_ffn_w_gate, dtype=np.int32)
    wu = np.asarray(L0_ffn_w_up, dtype=np.int32)
    wd = np.asarray(L0_ffn_w_down, dtype=np.int32)

    j = np.arange(3 * D)
    hh = j // (3 * DH)
    rr = j % (3 * DH)
    dd = rr // 3
    ccp = rr % 3
    jp = ccp * D + hh * DH + dd
    W2 = np.empty_like(W)
    W2[jp] = W
    b2 = np.empty_like(b)
    b2[jp] = b

    w2t = np.ascontiguousarray(W2.T)
    wot = np.ascontiguousarray(Wo.T)
    wgt = np.ascontiguousarray(wg.T)
    wut = np.ascontiguousarray(wu.T)
    wdt = np.ascontiguousarray(wd.T)
    ones = np.ones((1, 512), np.float32)
    onescol = np.ones((128, 1), np.float32)
    zrow = np.zeros((1, 128), np.float32)
    ii, jj2 = np.indices((128, 128))
    mask = np.where(ii > jj2, NEG, 0.0).astype(np.float32)
    ccols = np.tile(np.array([[-SHIFT, QSCALE]], np.float32), (128, 1))

    if _NC_CACHE is None:
        _NC_CACHE = build_nc()
    nc = _NC_CACHE

    in_maps = []
    for bidx in range(8):
        in_maps.append({
            "xT": np.ascontiguousarray(x[bidx].T),
            "w2t": w2t,
            "b2": b2.reshape(1, -1),
            "wot": wot,
            "bo": bo.reshape(1, -1),
            "wgt": wgt,
            "wut": wut,
            "wdt": wdt,
            "ones": ones,
            "onescol": onescol,
            "zrow": zrow,
            "mask": mask,
            "ccols": ccols,
        })
    res = run_bass_kernel_spmd(nc, in_maps, core_ids=list(range(8)))
    out = np.stack([np.ascontiguousarray(r["outT"].T) for r in res.results])
    return out.astype(np.float32)
